# version 17
# baseline (speedup 1.0000x reference)
import numpy as np
from collections import deque
from contextlib import ExitStack

import concourse.bass as bass
import concourse.mybir as mybir
from concourse.bass_utils import run_bass_kernel_spmd

F32 = mybir.dt.float32
BF16 = mybir.dt.bfloat16
AF = mybir.ActivationFunctionType
ALU = mybir.AluOpType

LN2 = 0.6931471805599453
TINY = 1.1754943508222875e-38
LAM_HI = 1.0 - 1e-9
EPS = 1e-5


class CFG:
    def __init__(self, NB, QPB, D, TPC, HID):
        self.NB = NB
        self.QPB = QPB
        self.NCORES = NB * QPB
        self.D = D
        self.NK = D // 128
        self.NH = D // 128
        self.TPC = TPC
        self.TH = TPC // 2
        self.SEG = min(512, self.TH)
        self.NSEG = self.TH // self.SEG
        self.HID = HID
        self.NMH = HID // 128
        self.QH = 4
        self.HQC = self.NMH // self.QH
        self.PREV = (QPB - 1) * TPC
        self.NPJ = self.PREV // 128
        self.L = QPB * TPC
        assert self.D <= 2 * self.TH


FULL = CFG(NB=2, QPB=4, D=2048, TPC=2048, HID=8192)
MINI = CFG(NB=1, QPB=2, D=256, TPC=256, HID=512)
MINI1 = CFG(NB=2, QPB=1, D=256, TPC=256, HID=512)


def build_program(cfg):
    c = cfg
    nc = bass.Bass("TRN2", target_bir_lowering=False)

    xT = nc.dram_tensor("xT", [c.D, c.TPC], F32, kind="ExternalInput").ap()
    if c.PREV:
        xprev = nc.dram_tensor("xprev", [c.PREV, c.D], F32,
                               kind="ExternalInput").ap()
    wq = nc.dram_tensor("wq", [c.NK, 128, c.D], F32, kind="ExternalInput").ap()
    wv = nc.dram_tensor("wv", [c.NK, 128, c.D], F32, kind="ExternalInput").ap()
    wg = nc.dram_tensor("wg", [c.NK, 128, c.D], F32, kind="ExternalInput").ap()
    wo = nc.dram_tensor("wo", [c.NK, 128, c.D], F32, kind="ExternalInput").ap()
    upw = nc.dram_tensor("upw", [c.NMH, 128, c.D], F32, kind="ExternalInput").ap()
    dnw = nc.dram_tensor("dnw", [c.NK, c.QH, 128, c.HQC * 128], F32,
                         kind="ExternalInput").ap()
    gam_d = nc.dram_tensor("gam", [128, c.NK], F32, kind="ExternalInput").ap()
    bet_d = nc.dram_tensor("bet", [128, c.NK], F32, kind="ExternalInput").ap()
    usc_d = nc.dram_tensor("usc", [128, c.NMH], F32, kind="ExternalInput").ap()
    dsc_d = nc.dram_tensor("dsc", [128, c.NK], F32, kind="ExternalInput").ap()
    osc_d = nc.dram_tensor("osc", [1, 1], F32, kind="ExternalInput").ap()
    betab = nc.dram_tensor("betab", [1, c.NH], F32, kind="ExternalInput").ap()
    iota_d = nc.dram_tensor("iota", [128, 1], F32, kind="ExternalInput").ap()
    iotar_d = nc.dram_tensor("iotar", [1, 128], F32, kind="ExternalInput").ap()
    ident_d = nc.dram_tensor("ident", [128, c.NH], F32, kind="ExternalInput").ap()
    OUT = nc.dram_tensor("OUT", [c.D, c.TPC], F32, kind="ExternalOutput").ap()

    ctx = ExitStack()
    sbuf = lambda n, s, dt=F32: ctx.enter_context(nc.sbuf_tensor(n, s, dt))
    psum = lambda n, s: ctx.enter_context(nc.psum_tensor(n, s, F32))

    TH, D, NK, NH, SEG, NSEG = c.TH, c.D, c.NK, c.NH, c.SEG, c.NSEG
    acc = sbuf("acc", [128, NK * TH])
    bvg = sbuf("bvg", [128, NK * TH], BF16)
    bxh = sbuf("bxh", [128, max(NK, c.HQC) * TH], BF16)
    SLOTW = 1024
    wsf = sbuf("wsf", [128, 5 * SLOTW])
    WMAX = max(D, c.HQC * 128)
    wsb = sbuf("wsb", [128, 3 * WMAX], BF16)
    XW = max(TH, min(D, 1024))
    xst = sbuf("xst", [128, 2 * XW])
    gsl = sbuf("gsl", [128, 2 * TH], BF16)
    ret = sbuf("ret", [128, TH])
    y2a = sbuf("y2a", [128, TH])
    sc8 = sbuf("sc8", [128, 2 * TH])
    rows = sbuf("rows", [128, 4 * TH])
    lam_bc = sbuf("lam_bc", [128, NH])
    llam_bc = sbuf("llam_bc", [128, NH])
    osbc = sbuf("osbc", [128, 1])
    dsos = sbuf("dsos", [128, NK])
    gam = sbuf("gam_s", [128, NK])
    bet = sbuf("bet_s", [128, NK])
    usc = sbuf("usc_s", [128, c.NMH])
    dsc = sbuf("dsc_s", [128, NK])
    stc = sbuf("stc", [128, NH])
    ones_row = sbuf("ones_row", [1, 128])
    ones_col_bf = sbuf("ones_col_bf", [128, 1], BF16)
    lamcat = sbuf("lamcat", [1, 2 * NH + 1])
    beta_row = sbuf("beta_row", [1, NH])
    iota_col = sbuf("iota_col", [128, 1])
    iota_row = sbuf("iota_row", [1, 128])
    ident = sbuf("ident_s", [128, NH])
    xdtb = sbuf("xdtb", [128, NK * NH], BF16) if c.PREV else None
    ecl = sbuf("ecl", [128, c.NPJ]) if c.PREV else None

    P = [psum(f"P{i}", [128, TH]) for i in range(4)]

    rA = rows[0:1, 0 * TH:1 * TH]
    rB = rows[0:1, 1 * TH:2 * TH]
    rC = rows[0:1, 2 * TH:3 * TH]
    rD = rows[0:1, 3 * TH:4 * TH]
    neg_bc = sc8[:, 0:TH]
    r_bc = sc8[:, TH:2 * TH]

    ENGS = ("sync", "gpsimd", "vector", "scalar", "tensor")
    cmds = {e: [] for e in ENGS}
    last_wait = {e: {} for e in ENGS}

    class Sem:
        def __init__(self, name):
            self.h = ctx.enter_context(nc.semaphore(name))
            self.name = name
            self.n = 0

    sSET = Sem("sSET")
    sW = [Sem(f"sW{i}") for i in range(5)]
    sX = [Sem(f"sX{i}") for i in range(2)]
    sO = Sem("sO")
    cT = Sem("cT")
    cV = Sem("cV")
    cS = Sem("cS")
    cG = Sem("cG")

    def emit(eng, fn):
        cmds[eng].append(fn)

    def wait(eng, sem, val):
        if val is None or val <= 0:
            return
        if val <= last_wait[eng].get(sem.name, 0):
            return
        last_wait[eng][sem.name] = val
        emit(eng, lambda e, s=sem.h, v=val: e.wait_ge(s, v))

    def inc(eng, sem, fn, k=1):
        sem.n += k
        emit(eng, lambda e, f=fn, s=sem.h, kk=k: f(e).then_inc(s, kk))
        return sem.n

    def run(eng, fn):
        emit(eng, fn)

    def drain(eng):
        emit(eng, lambda e: e.drain())

    class Pool:
        def __init__(self, n, base=0):
            self.evt = [None] * n
            self.i = 0
            self.n = n
            self.base = base

        def acquire(self, engs):
            if isinstance(engs, str):
                engs = (engs,)
            idx = self.i % self.n
            ev = self.evt[idx]
            if ev is not None:
                for eng in engs:
                    wait(eng, ev[0], ev[1])
            self.i += 1
            return self.base + idx

        def release(self, idx, sem, val):
            self.evt[idx - self.base] = (sem, val)

    wsf_pool = Pool(5)
    wsb_pool = Pool(3)
    xst_pool = Pool(2)
    gsl_pool = Pool(2)
    pA = Pool(2, base=0)
    pB = Pool(2, base=2)

    def stream_slab(src_ap, width, kind):
        nsub = (width + SLOTW - 1) // SLOTW
        if kind == "cast":
            cengs = ["vector" if (s % 2 == 0) else "scalar"
                     for s in range(nsub)]
        else:
            cengs = ["scalar"] * nsub
        bi = wsb_pool.acquire(tuple(dict.fromkeys(cengs)))
        waits = []
        for s in range(nsub):
            ceng = cengs[s]
            csem = {"gpsimd": cG, "scalar": cS, "vector": cV}[ceng]
            w0 = s * SLOTW
            w1 = min(width, w0 + SLOTW)
            si = wsf_pool.acquire("sync")
            v = inc("sync", sW[si],
                    lambda e, si=si, w0=w0, w1=w1, src=src_ap:
                    e.dma_start(out=wsf[:, si * SLOTW: si * SLOTW + (w1 - w0)],
                                in_=src[:, w0:w1]), 16)
            wait(ceng, sW[si], v)
            dst = wsb[:, bi * WMAX + w0: bi * WMAX + w1]
            srcs = wsf[:, si * SLOTW: si * SLOTW + (w1 - w0)]
            if kind == "cast" and ceng in ("gpsimd", "vector"):
                rdy = inc(ceng, csem,
                          lambda e, d=dst, s2=srcs: e.tensor_copy(d, s2))
            elif kind == "cast":
                rdy = inc(ceng, csem,
                          lambda e, d=dst, s2=srcs: e.copy(d, s2))
            else:
                rdy = inc(ceng, csem,
                          lambda e, d=dst, s2=srcs:
                          e.activation(d, s2, AF.Sign))
            wsf_pool.release(si, csem, rdy)
            waits.append((csem, rdy))
        return bi, waits

    slab_seq = []
    slab_state = {"idx": 0, "pend": deque()}

    def slab_take():
        st = slab_state
        if not st["pend"]:
            st["pend"].append(stream_slab(*slab_seq[st["idx"]]))
            st["idx"] += 1
        h = st["pend"].popleft()
        while len(st["pend"]) < 2 and st["idx"] < len(slab_seq):
            st["pend"].append(stream_slab(*slab_seq[st["idx"]]))
            st["idx"] += 1
        return h

    def slab_block(bi, k):
        return wsb[:, bi * WMAX + k * 128: bi * WMAX + (k + 1) * 128]

    def mm_group(pi, bi, rhs_tensor, rhs_base, nk_contract, kwaits=None):
        for k in range(nk_contract):
            if kwaits is not None:
                wait("tensor", kwaits[k][0], kwaits[k][1])
            for n in range(NSEG):
                final = (k == nk_contract - 1) and (n == NSEG - 1)
                lhsT = slab_block(bi, k)
                rr = rhs_tensor[:, rhs_base + k * TH + n * SEG:
                                rhs_base + k * TH + (n + 1) * SEG]
                oo = (pi, n)
                if final:
                    t = inc("tensor", cT,
                            lambda e, lhsT=lhsT, rr=rr, pi=pi, n=n, k=k:
                            e.matmul(P[pi][:, n * SEG:(n + 1) * SEG], lhsT, rr,
                                     start=(k == 0), stop=(k == nk_contract - 1)))
                else:
                    run("tensor",
                        lambda e, lhsT=lhsT, rr=rr, pi=pi, n=n, k=k,
                        nkc=nk_contract:
                        e.matmul(P[pi][:, n * SEG:(n + 1) * SEG], lhsT, rr,
                                 start=(k == 0), stop=(k == nkc - 1)))
        return t

    def ln_stats_rows(sb):
        run("vector", lambda e: e.tensor_scalar(rC, P[sb][0:1, 0:TH], 1.0 / D,
                                                None, ALU.mult))
        run("vector", lambda e: e.tensor_scalar(rD, P[sb + 1][0:1, 0:TH],
                                                1.0 / D, EPS, ALU.mult, ALU.add))
        drain("vector")
        run("vector", lambda e: e.tensor_tensor(rA, rC, rC, ALU.mult))
        drain("vector")
        v = inc("vector", cV, lambda e: e.scalar_tensor_tensor(rD, rA, -1.0, rD,
                                                               ALU.mult,
                                                               ALU.add))
        wait("scalar", cV, v)
        run("scalar", lambda e: e.activation(rA, rD, AF.Ln))
        drain("scalar")
        s = inc("scalar", cS, lambda e: e.activation(rA, rA, AF.Exp,
                                                     scale=-0.5))
        wait("vector", cS, s)
        v = inc("vector", cV, lambda e: e.scalar_tensor_tensor(rB, rC, -1.0, rA,
                                                               ALU.mult,
                                                               ALU.mult))
        return v

    def ln_bcast(rows_v):
        wait("tensor", cV, rows_v)
        i2 = pB.acquire("tensor")
        i3 = pB.acquire("tensor")
        t = 0
        for n in range(NSEG):
            run("tensor",
                lambda e, n=n, i2=i2:
                e.matmul(P[i2][:, n * SEG:(n + 1) * SEG], ones_row[0:1, :],
                         rB[:, n * SEG:(n + 1) * SEG], start=True, stop=True))
            final = (n == NSEG - 1)
            if final:
                t = inc("tensor", cT,
                        lambda e, n=n, i3=i3:
                        e.matmul(P[i3][:, n * SEG:(n + 1) * SEG],
                                 ones_row[0:1, :], rA[:, n * SEG:(n + 1) * SEG],
                                 start=True, stop=True))
            else:
                run("tensor",
                    lambda e, n=n, i3=i3:
                    e.matmul(P[i3][:, n * SEG:(n + 1) * SEG], ones_row[0:1, :],
                             rA[:, n * SEG:(n + 1) * SEG], start=True, stop=True))
        wait("vector", cT, t)
        run("vector", lambda e, i2=i2: e.tensor_copy(neg_bc, P[i2][:, 0:TH]))
        v = inc("vector", cV,
                lambda e, i3=i3: e.tensor_copy(r_bc, P[i3][:, 0:TH]))
        pB.release(i2, cV, v)
        pB.release(i3, cV, v)
        drain("vector")
        return v

    def stats_contrib(sb, m, first, last, gate_val):
        a = acc[:, m * TH:(m + 1) * TH]
        gi0 = gsl_pool.acquire("vector")
        v = inc("vector", cV,
                lambda e, gi0=gi0, a=a:
                e.tensor_copy(gsl[:, gi0 * TH:(gi0 + 1) * TH], a))
        gi1 = gsl_pool.acquire("scalar")
        wait("scalar", cV, gate_val)
        s = inc("scalar", cS,
                lambda e, gi1=gi1, a=a:
                e.activation(gsl[:, gi1 * TH:(gi1 + 1) * TH], a, AF.Square))
        wait("tensor", cV, v)
        wait("tensor", cS, s)
        t0 = t1 = 0
        for n in range(NSEG):
            final = (n == NSEG - 1)
            if final:
                t0 = inc("tensor", cT,
                         lambda e, n=n, gi0=gi0, sb=sb:
                         e.matmul(P[sb][0:1, n * SEG:(n + 1) * SEG],
                                  ones_col_bf[:, 0:1],
                                  gsl[:, gi0 * TH + n * SEG:
                                      gi0 * TH + (n + 1) * SEG],
                                  start=first, stop=last))
                t1 = inc("tensor", cT,
                         lambda e, n=n, gi1=gi1, sb=sb:
                         e.matmul(P[sb + 1][0:1, n * SEG:(n + 1) * SEG],
                                  ones_col_bf[:, 0:1],
                                  gsl[:, gi1 * TH + n * SEG:
                                      gi1 * TH + (n + 1) * SEG],
                                  start=first, stop=last))
            else:
                run("tensor",
                    lambda e, n=n, gi0=gi0, sb=sb, first=first, last=last:
                    e.matmul(P[sb][0:1, n * SEG:(n + 1) * SEG],
                             ones_col_bf[:, 0:1],
                             gsl[:, gi0 * TH + n * SEG: gi0 * TH + (n + 1) * SEG],
                             start=first, stop=last))
                run("tensor",
                    lambda e, n=n, gi1=gi1, sb=sb, first=first, last=last:
                    e.matmul(P[sb + 1][0:1, n * SEG:(n + 1) * SEG],
                             ones_col_bf[:, 0:1],
                             gsl[:, gi1 * TH + n * SEG: gi1 * TH + (n + 1) * SEG],
                             start=first, stop=last))
        gsl_pool.release(gi0, cT, t0)
        gsl_pool.release(gi1, cT, t1)

    inc("gpsimd", sSET, lambda e: e.dma_start(out=beta_row[:], in_=betab), 16)
    inc("gpsimd", sSET, lambda e: e.dma_start(out=iota_col[:], in_=iota_d), 16)
    inc("gpsimd", sSET, lambda e: e.dma_start(out=iota_row[:], in_=iotar_d), 16)
    inc("gpsimd", sSET, lambda e: e.dma_start(out=gam[:], in_=gam_d), 16)
    inc("gpsimd", sSET, lambda e: e.dma_start(out=bet[:], in_=bet_d), 16)
    inc("gpsimd", sSET, lambda e: e.dma_start(out=usc[:], in_=usc_d), 16)
    inc("gpsimd", sSET, lambda e: e.dma_start(out=dsc[:], in_=dsc_d), 16)
    inc("gpsimd", sSET, lambda e: e.dma_start(out=ident[:], in_=ident_d), 16)
    setup_lg = inc("gpsimd", sSET,
                   lambda e: e.dma_start(out=lamcat[0:1, 2 * NH:2 * NH + 1],
                                         in_=osc_d), 16)

    wait("vector", sSET, setup_lg)
    run("vector", lambda e: e.memset(ones_row[:], 1.0))
    run("vector", lambda e: e.memset(ones_col_bf[:], 1.0))
    v_ones = inc("vector", cV, lambda e: e.memset(stc[:], 0.0))

    wait("scalar", sSET, setup_lg)
    s = inc("scalar", cS,
            lambda e: e.activation(lamcat[0:1, 0:NH], beta_row[:], AF.Exp,
                                   scale=-LN2))
    wait("vector", cS, s)
    run("vector", lambda e: e.tensor_scalar(lamcat[0:1, 0:NH], lamcat[0:1, 0:NH],
                                            -1.0, 1.0, ALU.mult, ALU.add))
    drain("vector")
    run("vector", lambda e: e.tensor_scalar_max(lamcat[0:1, 0:NH],
                                                lamcat[0:1, 0:NH], TINY))
    drain("vector")
    v = inc("vector", cV, lambda e: e.tensor_scalar_min(lamcat[0:1, 0:NH],
                                                        lamcat[0:1, 0:NH],
                                                        LAM_HI))
    wait("scalar", cV, v)
    s = inc("scalar", cS,
            lambda e: e.activation(lamcat[0:1, NH:2 * NH], lamcat[0:1, 0:NH],
                                   AF.Ln))
    wait("tensor", cS, s)
    wait("tensor", cV, v_ones)
    i0 = pA.acquire("tensor")
    t = inc("tensor", cT,
            lambda e, i0=i0: e.matmul(P[i0][:, 0:2 * NH + 1], ones_row[0:1, :],
                                      lamcat[0:1, :], start=True, stop=True))
    wait("vector", cT, t)
    run("vector", lambda e, i0=i0: e.tensor_copy(lam_bc[:], P[i0][:, 0:NH]))
    run("vector", lambda e, i0=i0: e.tensor_copy(llam_bc[:], P[i0][:, NH:2 * NH]))
    v = inc("vector", cV,
            lambda e, i0=i0: e.tensor_copy(osbc[:], P[i0][:, 2 * NH:2 * NH + 1]))
    pA.release(i0, cV, v)
    drain("vector")
    inc("vector", cV,
        lambda e: e.tensor_tensor(dsos[:], dsc[:],
                                  osbc[:, 0:1].broadcast_to((128, NK)),
                                  ALU.mult))
    drain("vector")

    xdt_done_v = None
    if c.PREV:
        NPJ = c.NPJ
        XPW = min(D, 1024, TH)
        nxs = D // XPW
        dwt_all = sc8[:, 0:NPJ * NH].rearrange("p (j h) -> p j h", h=NH)
        dwtb_all = rows[:, 0:(NPJ * NH + 1) // 2].bitcast(BF16)
        jrow = rows[0:1, 2 * TH:2 * TH + NPJ]
        v = inc("vector", cV,
                lambda e: e.tensor_scalar(jrow, iota_row[0:1, 0:NPJ], -128.0,
                                          float(c.PREV - 1), ALU.mult, ALU.add))
        wait("tensor", cV, v)
        i0 = pA.acquire("tensor")
        t = inc("tensor", cT,
                lambda e, i0=i0: e.matmul(P[i0][:, 0:NPJ], ones_row[0:1, :],
                                          jrow, start=True, stop=True))
        wait("vector", cT, t)
        run("vector", lambda e, i0=i0: e.tensor_copy(ecl[:], P[i0][:, 0:NPJ]))
        drain("vector")
        run("vector",
            lambda e: e.tensor_tensor(ecl[:], ecl[:],
                                      iota_col[:, 0:1].broadcast_to((128, NPJ)),
                                      ALU.subtract))
        drain("vector")
        v = inc("vector", cV,
                lambda e: e.tensor_tensor(
                    dwt_all,
                    ecl[:].unsqueeze(2).broadcast_to((128, NPJ, NH)),
                    llam_bc[:].unsqueeze(1).broadcast_to((128, NPJ, NH)),
                    ALU.mult))
        pA.release(i0, cV, v)
        wait("scalar", cV, v)
        s2 = inc("scalar", cS,
                 lambda e: e.activation(sc8[:, 0:NPJ * NH], sc8[:, 0:NPJ * NH],
                                        AF.Exp))
        wait("vector", cS, s2)
        vdw = inc("vector", cV,
                  lambda e: e.tensor_copy(rows[:, 0:(NPJ * NH + 1) // 2]
                                          .bitcast(BF16), sc8[:, 0:NPJ * NH]))
        drain("vector")

        CW = min(512, XPW)
        npc = XPW // CW
        xps_slots = [sc8[:, 0:XPW], sc8[:, TH:TH + XPW],
                     y2a[:, 0:XPW], ret[:, 0:XPW]]
        sP = [Sem(f"sP{i}") for i in range(4)]
        xpf_pool = Pool(4)
        xpb_pool = Pool(min(4, (2 * TH) // CW))
        i2 = pB.acquire("tensor")
        i3 = pB.acquire("tensor") if D > TH else i2
        cst = {"tl": 0}

        def carry_emit_j(j):
            wait("gpsimd", cV, vdw)
            wait("tensor", cV, vdw)
            for sub in range(nxs):
                si = xpf_pool.acquire("gpsimd")
                inc("gpsimd", sP[si],
                    lambda e, si=si, j=j, sub=sub:
                    e.dma_start(out=xps_slots[si],
                                in_=xprev[j * 128:(j + 1) * 128,
                                          sub * XPW:(sub + 1) * XPW]), 16)
                g = sP[si].n
                for p in range(npc):
                    ceng = "gpsimd" if (sub * npc + p) % 2 == 0 else "vector"
                    csem = cG if ceng == "gpsimd" else cV
                    wait(ceng, sP[si], g)
                    bi = xpb_pool.acquire(ceng)
                    if ceng == "gpsimd":
                        v = inc("gpsimd", cG,
                                lambda e, si=si, bi=bi, p=p:
                                e.tensor_copy(gsl[:, bi * CW:(bi + 1) * CW],
                                              xps_slots[si][:, p * CW:
                                                            (p + 1) * CW]))
                    else:
                        v = inc("vector", cV,
                                lambda e, si=si, bi=bi, p=p:
                                e.tensor_copy(gsl[:, bi * CW:(bi + 1) * CW],
                                              xps_slots[si][:, p * CW:
                                                            (p + 1) * CW]))
                    if p == npc - 1:
                        xpf_pool.release(si, csem, v)
                    col0 = sub * XPW + p * CW
                    pi = i2 if col0 < TH else i3
                    po = col0 % TH
                    wait("tensor", csem, v)
                    t = inc("tensor", cT,
                            lambda e, j=j, bi=bi, pi=pi, po=po:
                            e.matmul(P[pi][0:NH, po:po + CW],
                                     dwtb_all[:, j * NH:(j + 1) * NH],
                                     gsl[:, bi * CW:(bi + 1) * CW],
                                     start=(j == 0), stop=(j == NPJ - 1)))
                    xpb_pool.release(bi, cT, t)
                    cst["tl"] = t

        def carry_finish():
            wait("vector", cT, cst["tl"])
            for piece in range((D + TH - 1) // TH):
                w = min(TH, D - piece * TH)
                pi = i2 if piece == 0 else i3
                v = inc("vector", cV,
                        lambda e, piece=piece, w=w, pi=pi:
                        e.tensor_copy(acc[0:NH, piece * TH: piece * TH + w],
                                      P[pi][0:NH, 0:w]))
            pB.release(i2, cV, v)
            if i3 != i2:
                pB.release(i3, cV, v)
            drain("vector")
            wait("tensor", cV, v)
            for k in range(NK):
                pi = pA.acquire("tensor")
                t = inc("tensor", cT,
                        lambda e, k=k, pi=pi:
                        e.transpose(P[pi][:, 0:NH],
                                    acc[0:NH, k * 128:(k + 1) * 128],
                                    ident[0:NH, 0:NH]))
                wait("vector", cT, t)
                v = inc("vector", cV,
                        lambda e, k=k, pi=pi:
                        e.tensor_copy(xdtb[:, k * NH:(k + 1) * NH],
                                      P[pi][:, 0:NH]))
                pA.release(pi, cV, v)
            return v

    for _half in range(2):
        for m in range(NK):
            slab_seq.append((wv[m], D, "cast"))
        if c.PREV and _half == 0:
            for mj in range(NK):
                slab_seq.append((wv[mj], D, "cast"))
        for m in range(NK):
            slab_seq.append((wq[m], D, "cast"))
        for m in range(NK):
            slab_seq.append((wg[m], D, "cast"))
        for m in range(NK):
            slab_seq.append((wo[m], D, "cast"))
        for qh in range(c.QH):
            for mh in range(c.HQC):
                slab_seq.append((upw[qh * c.HQC + mh], D, "sign"))
            for md in range(NK):
                slab_seq.append((dnw[md, qh], c.HQC * 128, "sign"))

    def emit_xtb(T0, dst, gate_ct=None, eng="scalar"):
        csem = cS if eng == "scalar" else cG
        vals = []
        for m in range(NK):
            xi = xst_pool.acquire("gpsimd")
            g = inc("gpsimd", sX[xi],
                    lambda e, xi=xi, m=m, T0=T0:
                    e.dma_start(out=xst[:, xi * XW: xi * XW + TH],
                                in_=xT[m * 128:(m + 1) * 128, T0:T0 + TH]), 16)
            if m == 0 and gate_ct is not None:
                wait(eng, cT, gate_ct)
            wait(eng, sX[xi], g)
            if eng == "scalar":
                v = inc("scalar", cS,
                        lambda e, xi=xi, m=m, dst=dst:
                        e.copy(dst[:, m * TH:(m + 1) * TH],
                               xst[:, xi * XW: xi * XW + TH]))
            else:
                v = inc("gpsimd", cG,
                        lambda e, xi=xi, m=m, dst=dst:
                        e.tensor_copy(dst[:, m * TH:(m + 1) * TH],
                                      xst[:, xi * XW: xi * XW + TH]))
            xst_pool.release(xi, csem, v)
            vals.append((csem, v))
        return vals

    out_final = 0
    hoisted_xtb = None
    for half in range(2):
        T0 = half * TH
        xb = bxh if half % 2 == 0 else bvg
        vb = bvg if half % 2 == 0 else bxh

        if half == 1 and hoisted_xtb is not None:
            xtb_vals = hoisted_xtb
        else:
            xtb_vals = emit_xtb(T0, xb)
        xtb_kwaits = xtb_vals

        for m in range(NK):
            bi, swaits = slab_take()
            pi = pA.acquire("tensor")
            for sem, val in swaits:
                wait("tensor", sem, val)
            t = mm_group(pi, bi, xb, 0, NK,
                         kwaits=xtb_kwaits if m == 0 else None)
            wsb_pool.release(bi, cT, t)
            if c.PREV and half == 0:
                for j in range(m * c.NPJ // NK, (m + 1) * c.NPJ // NK):
                    carry_emit_j(j)
            wait("vector", cT, t)
            v = inc("vector", cV,
                    lambda e, m=m, pi=pi, vb=vb:
                    e.tensor_copy(vb[:, m * TH:(m + 1) * TH], P[pi][:, 0:TH]))
            pA.release(pi, cV, v)

        if c.PREV and half == 0:
            xdt_done_v = carry_finish()
            wait("tensor", cV, xdt_done_v)
            for mj in range(NK):
                bi, swaits = slab_take()
                pi = pB.acquire("tensor")
                for sem, val in swaits:
                    wait("tensor", sem, val)
                t = 0
                for k in range(NK):
                    final = (k == NK - 1)
                    if final:
                        t = inc("tensor", cT,
                                lambda e, bi=bi, k=k, pi=pi:
                                e.matmul(P[pi][:, 0:NH], slab_block(bi, k),
                                         xdtb[:, k * NH:(k + 1) * NH],
                                         start=(k == 0), stop=True))
                    else:
                        run("tensor",
                            lambda e, bi=bi, k=k, pi=pi:
                            e.matmul(P[pi][:, 0:NH], slab_block(bi, k),
                                     xdtb[:, k * NH:(k + 1) * NH],
                                     start=(k == 0), stop=False))
                wsb_pool.release(bi, cT, t)
                wait("vector", cT, t)
                v = inc("vector", cV,
                        lambda e, mj=mj, pi=pi:
                        e.tensor_copy(stc[:, mj:mj + 1], P[pi][:, mj:mj + 1]))
                pB.release(pi, cV, v)
            drain("vector")

        drain("vector")
        if half == 1:
            wait("vector", sO, out_final)
        stats_defer = None
        for m in range(NK):
            bi, swaits = slab_take()
            pi = pA.acquire("tensor")
            for sem, val in swaits:
                wait("tensor", sem, val)
            t = mm_group(pi, bi, xb, 0, NK)
            wsb_pool.release(bi, cT, t)
            if stats_defer is not None:
                stats_defer()
                stats_defer = None
            run("vector", lambda e, m=m, vb=vb:
                e.tensor_tensor_scan(ret[:],
                                     lam_bc[:, m:m + 1].broadcast_to((128, TH)),
                                     vb[:, m * TH:(m + 1) * TH],
                                     stc[:, m:m + 1], ALU.mult, ALU.add))
            drain("vector")
            run("vector", lambda e, m=m:
                e.tensor_copy(stc[:, m:m + 1], ret[:, TH - 1:TH]))
            wait("vector", cT, t)
            v = inc("vector", cV,
                    lambda e, m=m, pi=pi:
                    e.tensor_tensor(acc[:, m * TH:(m + 1) * TH], P[pi][:, 0:TH],
                                    ret[:], ALU.mult))
            pA.release(pi, cV, v)
            drain("vector")
            stats_defer = (lambda m=m, v=v:
                           stats_contrib(2, m, m == 0, m == NK - 1, v))
        stats_defer()
        wait("vector", cT, cT.n)
        rows_v = ln_stats_rows(2)
        bc_v = None

        pend = []

        def g_apply(m, gi, s_sil):
            wait("vector", cV, bc_v)
            run("vector", lambda e, m=m:
                e.tensor_tensor(ret[:], acc[:, m * TH:(m + 1) * TH], r_bc,
                                ALU.mult))
            drain("vector")
            v = inc("vector", cV,
                    lambda e: e.tensor_tensor(ret[:], ret[:], neg_bc, ALU.add))
            wait("scalar", cV, v)
            sg = inc("scalar", cS,
                     lambda e, m=m:
                     e.activation(y2a[:, 0:TH], ret[:], AF.Identity,
                                  bias=bet[:, m:m + 1], scale=gam[:, m:m + 1]))
            wait("vector", cS, sg)
            wait("vector", cS, s_sil)
            v = inc("vector", cV,
                    lambda e, m=m, gi=gi, vb=vb:
                    e.tensor_tensor(vb[:, m * TH:(m + 1) * TH], y2a[:, 0:TH],
                                    gsl[:, gi * TH:(gi + 1) * TH], ALU.mult))
            gsl_pool.release(gi, cV, v)
            drain("vector")

        for m in range(NK):
            bi, swaits = slab_take()
            pi = pA.acquire("tensor")
            for sem, val in swaits:
                wait("tensor", sem, val)
            t = mm_group(pi, bi, xb, 0, NK)
            wsb_pool.release(bi, cT, t)
            gi = gsl_pool.acquire("scalar")
            wait("scalar", cT, t)
            s_sil = inc("scalar", cS,
                        lambda e, gi=gi, pi=pi:
                        e.activation(gsl[:, gi * TH:(gi + 1) * TH],
                                     P[pi][:, 0:TH], AF.Silu))
            pA.release(pi, cS, s_sil)
            pend.append((m, gi, s_sil))
            if m == min(1, NK - 1):
                bc_v = ln_bcast(rows_v)
            if bc_v is not None:
                while pend:
                    g_apply(*pend.pop(0))

        stats_defer = None
        for m in range(NK):
            bi, swaits = slab_take()
            pi = pA.acquire("tensor")
            for sem, val in swaits:
                wait("tensor", sem, val)
            t = mm_group(pi, bi, vb, 0, NK)
            wsb_pool.release(bi, cT, t)
            if stats_defer is not None:
                stats_defer()
                stats_defer = None
            xi = xst_pool.acquire("gpsimd")
            g = inc("gpsimd", sX[xi],
                    lambda e, xi=xi, m=m, T0=T0:
                    e.dma_start(out=xst[:, xi * XW: xi * XW + TH],
                                in_=xT[m * 128:(m + 1) * 128, T0:T0 + TH]), 16)
            wait("vector", sX[xi], g)
            wait("vector", cT, t)
            v = inc("vector", cV,
                    lambda e, m=m, pi=pi, xi=xi:
                    e.tensor_tensor(acc[:, m * TH:(m + 1) * TH], P[pi][:, 0:TH],
                                    xst[:, xi * XW: xi * XW + TH], ALU.add))
            pA.release(pi, cV, v)
            xst_pool.release(xi, cV, v)
            drain("vector")
            stats_defer = (lambda m=m, v=v:
                           stats_contrib(2, m, m == 0, m == NK - 1, v))
        stats_defer()
        wait("vector", cT, cT.n)
        rows_v = ln_stats_rows(2)
        bc_v = ln_bcast(rows_v)

        wait("vector", cV, bc_v)
        sck = []
        for m in range(NK):
            run("vector", lambda e, m=m:
                e.tensor_tensor(ret[:], acc[:, m * TH:(m + 1) * TH], r_bc,
                                ALU.mult))
            drain("vector")
            v = inc("vector", cV,
                    lambda e, m=m:
                    e.tensor_tensor(acc[:, m * TH:(m + 1) * TH], ret[:],
                                    neg_bc, ALU.add))
            drain("vector")
            wait("scalar", cV, v)
            s = inc("scalar", cS,
                    lambda e, m=m, vb=vb:
                    e.copy(vb[:, m * TH:(m + 1) * TH],
                           acc[:, m * TH:(m + 1) * TH]))
            sck.append(s)
        y1_kwaits = [(cS, sck[k]) for k in range(NK)]

        first_ffn = True
        stats_defer = None
        for qh in range(c.QH):
            for mh in range(c.HQC):
                gmh = qh * c.HQC + mh
                bi, swaits = slab_take()
                pi = pA.acquire("tensor")
                for sem, val in swaits:
                    wait("tensor", sem, val)
                t = mm_group(pi, bi, vb, 0, NK,
                             kwaits=y1_kwaits if first_ffn else None)
                first_ffn = False
                wsb_pool.release(bi, cT, t)
                wait("scalar", cT, t)
                s = inc("scalar", cS,
                        lambda e, mh=mh, pi=pi, gmh=gmh, xb=xb:
                        e.activation(xb[:, mh * TH:(mh + 1) * TH],
                                     P[pi][:, 0:TH], AF.Relu,
                                     scale=usc[:, gmh:gmh + 1]))
                pA.release(pi, cS, s)
                if qh == c.QH - 1 and mh == c.HQC - 1:
                    last_up_t = t
            relu_s = s
            wait("tensor", cS, relu_s)
            for md in range(NK):
                bi, swaits = slab_take()
                pi = pB.acquire("tensor")
                for sem, val in swaits:
                    wait("tensor", sem, val)
                t = mm_group(pi, bi, xb, 0, c.HQC)
                wsb_pool.release(bi, cT, t)
                if stats_defer is not None:
                    stats_defer()
                    stats_defer = None
                wait("vector", cT, t)
                v = inc("vector", cV,
                        lambda e, md=md, pi=pi:
                        e.scalar_tensor_tensor(acc[:, md * TH:(md + 1) * TH],
                                               P[pi][:, 0:TH],
                                               dsos[:, md:md + 1],
                                               acc[:, md * TH:(md + 1) * TH],
                                               ALU.mult, ALU.add))
                pB.release(pi, cV, v)
                if qh == c.QH - 1:
                    drain("vector")
                    stats_defer = (lambda md=md, v=v:
                                   stats_contrib(0, md, md == 0, md == NK - 1,
                                                 v))
        stats_defer()
        if half == 0:
            hoisted_xtb = emit_xtb(TH, vb, gate_ct=last_up_t, eng="gpsimd")
        wait("vector", cT, cT.n)
        rows_v = ln_stats_rows(0)
        pA.evt = [(cV, rows_v), (cV, rows_v)]
        bc_v = ln_bcast(rows_v)

        wait("vector", cV, bc_v)
        for m in range(NK):
            run("vector", lambda e, m=m:
                e.tensor_tensor(ret[:], acc[:, m * TH:(m + 1) * TH], r_bc,
                                ALU.mult))
            drain("vector")
            v = inc("vector", cV,
                    lambda e, m=m:
                    e.tensor_tensor(acc[:, m * TH:(m + 1) * TH], ret[:],
                                    neg_bc, ALU.add))
            drain("vector")
            wait("gpsimd", cV, v)
            out_final = inc("gpsimd", sO,
                            lambda e, m=m, T0=T0:
                            e.dma_start(out=OUT[m * 128:(m + 1) * 128,
                                                T0:T0 + TH],
                                        in_=acc[:, m * TH:(m + 1) * TH]), 16)

    emit("gpsimd", lambda e: e.wait_ge(sO.h, sO.n))

    with nc.Block() as block:
        @block.sync
        def _(e):
            for f in cmds["sync"]:
                f(e)

        @block.gpsimd
        def _(e):
            for f in cmds["gpsimd"]:
                f(e)

        @block.vector
        def _(e):
            for f in cmds["vector"]:
                f(e)

        @block.scalar
        def _(e):
            for f in cmds["scalar"]:
                f(e)

        @block.tensor
        def _(e):
            for f in cmds["tensor"]:
                f(e)

    ctx.close()
    return nc


def _block_attn(w):
    D = w.shape[0]
    nk = D // 128
    wT = np.ascontiguousarray(w.T)
    return np.ascontiguousarray(
        wT.reshape(nk, 128, nk, 128).transpose(2, 1, 0, 3).reshape(nk, 128, D))


def _block_up(w, D, HID):
    nk, nmh = D // 128, HID // 128
    wT = np.ascontiguousarray(w.T)
    return np.ascontiguousarray(
        wT.reshape(nk, 128, nmh, 128).transpose(2, 1, 0, 3).reshape(nmh, 128, D))


def _block_dn(w, D, HID, QH):
    nk, hqc = D // 128, HID // 128 // QH
    wT = np.ascontiguousarray(w.T)
    return np.ascontiguousarray(
        wT.reshape(QH, hqc, 128, nk, 128).transpose(3, 0, 2, 1, 4)
        .reshape(nk, QH, 128, hqc * 128))


def make_in_maps(cfg, x, Wq, Wv, Wo, Wg, ln_gamma, ln_beta, beta,
                 up_w, down_w, up_scale, down_scale, out_scale):
    c = cfg
    f = np.float32
    com = {
        "wq": _block_attn(np.asarray(Wq, f)),
        "wv": _block_attn(np.asarray(Wv, f)),
        "wg": _block_attn(np.asarray(Wg, f)),
        "wo": _block_attn(np.asarray(Wo, f)),
        "upw": _block_up(np.asarray(up_w, f), c.D, c.HID),
        "dnw": _block_dn(np.asarray(down_w, f), c.D, c.HID, c.QH),
        "gam": np.ascontiguousarray(np.asarray(ln_gamma, f).reshape(c.NK, 128).T),
        "bet": np.ascontiguousarray(np.asarray(ln_beta, f).reshape(c.NK, 128).T),
        "usc": np.ascontiguousarray(np.asarray(up_scale, f).reshape(c.NMH, 128).T),
        "dsc": np.ascontiguousarray(np.asarray(down_scale, f).reshape(c.NK, 128).T),
        "osc": np.asarray(out_scale, f).reshape(1, 1),
        "betab": np.asarray(beta, f).reshape(1, c.NH),
        "iota": np.arange(128, dtype=f).reshape(128, 1),
        "iotar": np.arange(128, dtype=f).reshape(1, 128),
        "ident": np.ascontiguousarray(np.eye(128, dtype=f)[:, :c.NH]),
    }
    x = np.asarray(x, f)
    in_maps = []
    for core in range(c.NCORES):
        b, q = core // c.QPB, core % c.QPB
        t0 = q * c.TPC
        d = dict(com)
        d["xT"] = np.ascontiguousarray(x[b, t0:t0 + c.TPC, :].T)
        if c.PREV:
            xp = np.zeros((c.PREV, c.D), f)
            if t0:
                xp[c.PREV - t0:] = x[b, :t0]
            d["xprev"] = xp
        in_maps.append(d)
    return in_maps


def gather_out(cfg, results):
    c = cfg
    y = np.empty((c.NB, c.L, c.D), np.float32)
    for core in range(c.NCORES):
        b, q = core // c.QPB, core % c.QPB
        t0 = q * c.TPC
        y[b, t0:t0 + c.TPC, :] = results[core]["OUT"].T
    return y


def kernel(L0_attn_x, L0_attn_Wq, L0_attn_Wv, L0_attn_Wo, L0_attn_Wg,
           L0_attn_ln_gamma, L0_attn_ln_beta, L0_attn_beta,
           L0_ffn_up_weight, L0_ffn_down_weight, L0_ffn_up_scale,
           L0_ffn_down_scale, L0_ffn_output_scale):
    cfg = FULL
    nc = build_program(cfg)
    in_maps = make_in_maps(cfg, L0_attn_x, L0_attn_Wq, L0_attn_Wv, L0_attn_Wo,
                           L0_attn_Wg, L0_attn_ln_gamma, L0_attn_ln_beta,
                           L0_attn_beta, L0_ffn_up_weight, L0_ffn_down_weight,
                           L0_ffn_up_scale, L0_ffn_down_scale,
                           L0_ffn_output_scale)
    res = run_bass_kernel_spmd(nc, in_maps, core_ids=list(range(8)))
    return gather_out(cfg, list(res.results))


# version 18 (final)
# speedup vs baseline: 1.1807x; 1.1807x over previous
import numpy as np
from collections import deque
from contextlib import ExitStack

import concourse.bass as bass
import concourse.mybir as mybir
from concourse.bass_utils import run_bass_kernel_spmd

F32 = mybir.dt.float32
BF16 = mybir.dt.bfloat16
AF = mybir.ActivationFunctionType
ALU = mybir.AluOpType

LN2 = 0.6931471805599453
TINY = 1.1754943508222875e-38
LAM_HI = 1.0 - 1e-9
EPS = 1e-5


class CFG:
    def __init__(self, NB, QPB, D, TPC, HID):
        self.NB = NB
        self.QPB = QPB
        self.NCORES = NB * QPB
        self.D = D
        self.NK = D // 128
        self.NH = D // 128
        self.TPC = TPC
        self.TH = TPC // 2
        self.SEG = min(512, self.TH)
        self.NSEG = self.TH // self.SEG
        self.HID = HID
        self.NMH = HID // 128
        self.QH = 4
        self.HQC = self.NMH // self.QH
        self.PREV = (QPB - 1) * TPC
        self.NPJ = self.PREV // 128
        self.L = QPB * TPC
        assert self.D <= 2 * self.TH


FULL = CFG(NB=2, QPB=4, D=2048, TPC=2048, HID=8192)
MINI = CFG(NB=1, QPB=2, D=256, TPC=256, HID=512)
MINI1 = CFG(NB=2, QPB=1, D=256, TPC=256, HID=512)


def build_program(cfg):
    c = cfg
    nc = bass.Bass("TRN2", target_bir_lowering=False)

    xT = nc.dram_tensor("xT", [c.D, c.TPC], F32, kind="ExternalInput").ap()
    if c.PREV:
        xprev = nc.dram_tensor("xprev", [c.PREV, c.D], F32,
                               kind="ExternalInput").ap()
    wq = nc.dram_tensor("wq", [c.NK, 128, c.D], F32, kind="ExternalInput").ap()
    wv = nc.dram_tensor("wv", [c.NK, 128, c.D], F32, kind="ExternalInput").ap()
    wg = nc.dram_tensor("wg", [c.NK, 128, c.D], F32, kind="ExternalInput").ap()
    wo = nc.dram_tensor("wo", [c.NK, 128, c.D], F32, kind="ExternalInput").ap()
    upw = nc.dram_tensor("upw", [c.NMH, 128, c.D], F32, kind="ExternalInput").ap()
    dnw = nc.dram_tensor("dnw", [c.NK, c.QH, 128, c.HQC * 128], F32,
                         kind="ExternalInput").ap()
    gam_d = nc.dram_tensor("gam", [128, c.NK], F32, kind="ExternalInput").ap()
    bet_d = nc.dram_tensor("bet", [128, c.NK], F32, kind="ExternalInput").ap()
    usc_d = nc.dram_tensor("usc", [128, c.NMH], F32, kind="ExternalInput").ap()
    dsc_d = nc.dram_tensor("dsc", [128, c.NK], F32, kind="ExternalInput").ap()
    osc_d = nc.dram_tensor("osc", [1, 1], F32, kind="ExternalInput").ap()
    betab = nc.dram_tensor("betab", [1, c.NH], F32, kind="ExternalInput").ap()
    iota_d = nc.dram_tensor("iota", [128, 1], F32, kind="ExternalInput").ap()
    iotar_d = nc.dram_tensor("iotar", [1, 128], F32, kind="ExternalInput").ap()
    ident_d = nc.dram_tensor("ident", [128, c.NH], F32, kind="ExternalInput").ap()
    OUT = nc.dram_tensor("OUT", [c.D, c.TPC], F32, kind="ExternalOutput").ap()

    ctx = ExitStack()
    sbuf = lambda n, s, dt=F32: ctx.enter_context(nc.sbuf_tensor(n, s, dt))
    psum = lambda n, s: ctx.enter_context(nc.psum_tensor(n, s, F32))

    TH, D, NK, NH, SEG, NSEG = c.TH, c.D, c.NK, c.NH, c.SEG, c.NSEG
    acc = sbuf("acc", [128, NK * TH])
    bvg = sbuf("bvg", [128, NK * TH], BF16)
    bxh = sbuf("bxh", [128, max(NK, c.HQC) * TH], BF16)
    SLOTW = 1024
    wsf = sbuf("wsf", [128, 5 * SLOTW])
    WMAX = max(D, c.HQC * 128)
    wsb = sbuf("wsb", [128, 3 * WMAX], BF16)
    XW = max(TH, min(D, 1024))
    xst = sbuf("xst", [128, 2 * XW])
    gsl = sbuf("gsl", [128, 2 * TH], BF16)
    ret = sbuf("ret", [128, TH])
    y2a = sbuf("y2a", [128, TH])
    sc8 = sbuf("sc8", [128, 2 * TH])
    rows = sbuf("rows", [128, 4 * TH])
    lam_bc = sbuf("lam_bc", [128, NH])
    llam_bc = sbuf("llam_bc", [128, NH])
    osbc = sbuf("osbc", [128, 1])
    dsos = sbuf("dsos", [128, NK])
    gam = sbuf("gam_s", [128, NK])
    bet = sbuf("bet_s", [128, NK])
    usc = sbuf("usc_s", [128, c.NMH])
    dsc = sbuf("dsc_s", [128, NK])
    stc = sbuf("stc", [128, NH])
    ones_row = sbuf("ones_row", [1, 128])
    ones_col_bf = sbuf("ones_col_bf", [128, 1], BF16)
    lamcat = sbuf("lamcat", [1, 2 * NH + 1])
    beta_row = sbuf("beta_row", [1, NH])
    iota_col = sbuf("iota_col", [128, 1])
    iota_row = sbuf("iota_row", [1, 128])
    ident = sbuf("ident_s", [128, NH])
    xdtb = sbuf("xdtb", [128, NK * NH], BF16) if c.PREV else None
    ecl = sbuf("ecl", [128, c.NPJ]) if c.PREV else None

    P = [psum(f"P{i}", [128, TH]) for i in range(4)]

    rA = rows[0:1, 0 * TH:1 * TH]
    rB = rows[0:1, 1 * TH:2 * TH]
    rC = rows[0:1, 2 * TH:3 * TH]
    rD = rows[0:1, 3 * TH:4 * TH]
    neg_bc = sc8[:, 0:TH]
    r_bc = sc8[:, TH:2 * TH]

    ENGS = ("sync", "gpsimd", "vector", "scalar", "tensor")
    cmds = {e: [] for e in ENGS}
    last_wait = {e: {} for e in ENGS}

    class Sem:
        def __init__(self, name):
            self.h = ctx.enter_context(nc.semaphore(name))
            self.name = name
            self.n = 0

    sSET = Sem("sSET")
    sW = [Sem(f"sW{i}") for i in range(5)]
    sX = [Sem(f"sX{i}") for i in range(2)]
    sO = Sem("sO")
    cT = Sem("cT")
    cV = Sem("cV")
    cS = Sem("cS")
    cG = Sem("cG")

    def emit(eng, fn):
        cmds[eng].append(fn)

    def wait(eng, sem, val):
        if val is None or val <= 0:
            return
        if val <= last_wait[eng].get(sem.name, 0):
            return
        last_wait[eng][sem.name] = val
        emit(eng, lambda e, s=sem.h, v=val: e.wait_ge(s, v))

    def inc(eng, sem, fn, k=1):
        sem.n += k
        emit(eng, lambda e, f=fn, s=sem.h, kk=k: f(e).then_inc(s, kk))
        return sem.n

    def run(eng, fn):
        emit(eng, fn)

    def drain(eng):
        emit(eng, lambda e: e.drain())

    class Pool:
        def __init__(self, n, base=0):
            self.evt = [None] * n
            self.i = 0
            self.n = n
            self.base = base

        def acquire(self, engs):
            if isinstance(engs, str):
                engs = (engs,)
            idx = self.i % self.n
            ev = self.evt[idx]
            if ev is not None:
                for eng in engs:
                    wait(eng, ev[0], ev[1])
            self.i += 1
            return self.base + idx

        def release(self, idx, sem, val):
            self.evt[idx - self.base] = (sem, val)

    wsf_pool = Pool(5)
    wsb_pool = Pool(3)
    xst_pool = Pool(2)
    gsl_pool = Pool(2)
    pA = Pool(2, base=0)
    pB = Pool(2, base=2)

    def stream_slab(src_ap, width, kind):
        nsub = (width + SLOTW - 1) // SLOTW
        if kind == "cast":
            cengs = ["vector" if (s % 2 == 0) else "scalar"
                     for s in range(nsub)]
        else:
            cengs = ["scalar"] * nsub
        bi = wsb_pool.acquire(tuple(dict.fromkeys(cengs)))
        waits = []
        for s in range(nsub):
            ceng = cengs[s]
            csem = {"gpsimd": cG, "scalar": cS, "vector": cV}[ceng]
            w0 = s * SLOTW
            w1 = min(width, w0 + SLOTW)
            si = wsf_pool.acquire("sync")
            v = inc("sync", sW[si],
                    lambda e, si=si, w0=w0, w1=w1, src=src_ap:
                    e.dma_start(out=wsf[:, si * SLOTW: si * SLOTW + (w1 - w0)],
                                in_=src[:, w0:w1]), 16)
            wait(ceng, sW[si], v)
            dst = wsb[:, bi * WMAX + w0: bi * WMAX + w1]
            srcs = wsf[:, si * SLOTW: si * SLOTW + (w1 - w0)]
            if kind == "cast" and ceng in ("gpsimd", "vector"):
                rdy = inc(ceng, csem,
                          lambda e, d=dst, s2=srcs: e.tensor_copy(d, s2))
            elif kind == "cast":
                rdy = inc(ceng, csem,
                          lambda e, d=dst, s2=srcs: e.copy(d, s2))
            else:
                rdy = inc(ceng, csem,
                          lambda e, d=dst, s2=srcs:
                          e.activation(d, s2, AF.Sign))
            wsf_pool.release(si, csem, rdy)
            waits.append((csem, rdy))
        return bi, waits

    slab_seq = []
    slab_state = {"idx": 0, "pend": deque()}

    def slab_take():
        st = slab_state
        if not st["pend"]:
            st["pend"].append(stream_slab(*slab_seq[st["idx"]]))
            st["idx"] += 1
        h = st["pend"].popleft()
        while len(st["pend"]) < 2 and st["idx"] < len(slab_seq):
            st["pend"].append(stream_slab(*slab_seq[st["idx"]]))
            st["idx"] += 1
        return h

    def slab_block(bi, k):
        return wsb[:, bi * WMAX + k * 128: bi * WMAX + (k + 1) * 128]

    def mm_group(pi, bi, rhs_tensor, rhs_base, nk_contract, kwaits=None):
        for k in range(nk_contract):
            if kwaits is not None:
                wait("tensor", kwaits[k][0], kwaits[k][1])
            for n in range(NSEG):
                final = (k == nk_contract - 1) and (n == NSEG - 1)
                lhsT = slab_block(bi, k)
                rr = rhs_tensor[:, rhs_base + k * TH + n * SEG:
                                rhs_base + k * TH + (n + 1) * SEG]
                oo = (pi, n)
                if final:
                    t = inc("tensor", cT,
                            lambda e, lhsT=lhsT, rr=rr, pi=pi, n=n, k=k:
                            e.matmul(P[pi][:, n * SEG:(n + 1) * SEG], lhsT, rr,
                                     start=(k == 0), stop=(k == nk_contract - 1)))
                else:
                    run("tensor",
                        lambda e, lhsT=lhsT, rr=rr, pi=pi, n=n, k=k,
                        nkc=nk_contract:
                        e.matmul(P[pi][:, n * SEG:(n + 1) * SEG], lhsT, rr,
                                 start=(k == 0), stop=(k == nkc - 1)))
        return t

    def ln_stats_rows(sb):
        run("vector", lambda e: e.tensor_scalar(rC, P[sb][0:1, 0:TH], 1.0 / D,
                                                None, ALU.mult))
        run("vector", lambda e: e.tensor_scalar(rD, P[sb + 1][0:1, 0:TH],
                                                1.0 / D, EPS, ALU.mult, ALU.add))
        drain("vector")
        run("vector", lambda e: e.tensor_tensor(rA, rC, rC, ALU.mult))
        drain("vector")
        v = inc("vector", cV, lambda e: e.scalar_tensor_tensor(rD, rA, -1.0, rD,
                                                               ALU.mult,
                                                               ALU.add))
        wait("scalar", cV, v)
        run("scalar", lambda e: e.activation(rA, rD, AF.Ln))
        drain("scalar")
        s = inc("scalar", cS, lambda e: e.activation(rA, rA, AF.Exp,
                                                     scale=-0.5))
        wait("vector", cS, s)
        v = inc("vector", cV, lambda e: e.scalar_tensor_tensor(rB, rC, -1.0, rA,
                                                               ALU.mult,
                                                               ALU.mult))
        return v

    def ln_bcast(rows_v):
        wait("tensor", cV, rows_v)
        i2 = pB.acquire("tensor")
        i3 = pB.acquire("tensor")
        t = 0
        for n in range(NSEG):
            run("tensor",
                lambda e, n=n, i2=i2:
                e.matmul(P[i2][:, n * SEG:(n + 1) * SEG], ones_row[0:1, :],
                         rB[:, n * SEG:(n + 1) * SEG], start=True, stop=True))
            final = (n == NSEG - 1)
            if final:
                t = inc("tensor", cT,
                        lambda e, n=n, i3=i3:
                        e.matmul(P[i3][:, n * SEG:(n + 1) * SEG],
                                 ones_row[0:1, :], rA[:, n * SEG:(n + 1) * SEG],
                                 start=True, stop=True))
            else:
                run("tensor",
                    lambda e, n=n, i3=i3:
                    e.matmul(P[i3][:, n * SEG:(n + 1) * SEG], ones_row[0:1, :],
                             rA[:, n * SEG:(n + 1) * SEG], start=True, stop=True))
        wait("vector", cT, t)
        run("vector", lambda e, i2=i2: e.tensor_copy(neg_bc, P[i2][:, 0:TH]))
        v = inc("vector", cV,
                lambda e, i3=i3: e.tensor_copy(r_bc, P[i3][:, 0:TH]))
        pB.release(i2, cV, v)
        pB.release(i3, cV, v)
        drain("vector")
        return v

    def stats_contrib(sb, m, first, last, gate_val):
        a = acc[:, m * TH:(m + 1) * TH]
        gi0 = gsl_pool.acquire("vector")
        v = inc("vector", cV,
                lambda e, gi0=gi0, a=a:
                e.tensor_copy(gsl[:, gi0 * TH:(gi0 + 1) * TH], a))
        gi1 = gsl_pool.acquire("scalar")
        wait("scalar", cV, gate_val)
        s = inc("scalar", cS,
                lambda e, gi1=gi1, a=a:
                e.activation(gsl[:, gi1 * TH:(gi1 + 1) * TH], a, AF.Square))
        wait("tensor", cV, v)
        wait("tensor", cS, s)
        t0 = t1 = 0
        for n in range(NSEG):
            final = (n == NSEG - 1)
            if final:
                t0 = inc("tensor", cT,
                         lambda e, n=n, gi0=gi0, sb=sb:
                         e.matmul(P[sb][0:1, n * SEG:(n + 1) * SEG],
                                  ones_col_bf[:, 0:1],
                                  gsl[:, gi0 * TH + n * SEG:
                                      gi0 * TH + (n + 1) * SEG],
                                  start=first, stop=last))
                t1 = inc("tensor", cT,
                         lambda e, n=n, gi1=gi1, sb=sb:
                         e.matmul(P[sb + 1][0:1, n * SEG:(n + 1) * SEG],
                                  ones_col_bf[:, 0:1],
                                  gsl[:, gi1 * TH + n * SEG:
                                      gi1 * TH + (n + 1) * SEG],
                                  start=first, stop=last))
            else:
                run("tensor",
                    lambda e, n=n, gi0=gi0, sb=sb, first=first, last=last:
                    e.matmul(P[sb][0:1, n * SEG:(n + 1) * SEG],
                             ones_col_bf[:, 0:1],
                             gsl[:, gi0 * TH + n * SEG: gi0 * TH + (n + 1) * SEG],
                             start=first, stop=last))
                run("tensor",
                    lambda e, n=n, gi1=gi1, sb=sb, first=first, last=last:
                    e.matmul(P[sb + 1][0:1, n * SEG:(n + 1) * SEG],
                             ones_col_bf[:, 0:1],
                             gsl[:, gi1 * TH + n * SEG: gi1 * TH + (n + 1) * SEG],
                             start=first, stop=last))
        gsl_pool.release(gi0, cT, t0)
        gsl_pool.release(gi1, cT, t1)

    inc("gpsimd", sSET, lambda e: e.dma_start(out=beta_row[:], in_=betab), 16)
    inc("gpsimd", sSET, lambda e: e.dma_start(out=iota_col[:], in_=iota_d), 16)
    inc("gpsimd", sSET, lambda e: e.dma_start(out=iota_row[:], in_=iotar_d), 16)
    inc("gpsimd", sSET, lambda e: e.dma_start(out=gam[:], in_=gam_d), 16)
    inc("gpsimd", sSET, lambda e: e.dma_start(out=bet[:], in_=bet_d), 16)
    inc("gpsimd", sSET, lambda e: e.dma_start(out=usc[:], in_=usc_d), 16)
    inc("gpsimd", sSET, lambda e: e.dma_start(out=dsc[:], in_=dsc_d), 16)
    inc("gpsimd", sSET, lambda e: e.dma_start(out=ident[:], in_=ident_d), 16)
    setup_lg = inc("gpsimd", sSET,
                   lambda e: e.dma_start(out=lamcat[0:1, 2 * NH:2 * NH + 1],
                                         in_=osc_d), 16)

    wait("vector", sSET, setup_lg)
    run("vector", lambda e: e.memset(ones_row[:], 1.0))
    run("vector", lambda e: e.memset(ones_col_bf[:], 1.0))
    v_ones = inc("vector", cV, lambda e: e.memset(stc[:], 0.0))

    wait("scalar", sSET, setup_lg)
    s = inc("scalar", cS,
            lambda e: e.activation(lamcat[0:1, 0:NH], beta_row[:], AF.Exp,
                                   scale=-LN2))
    wait("vector", cS, s)
    run("vector", lambda e: e.tensor_scalar(lamcat[0:1, 0:NH], lamcat[0:1, 0:NH],
                                            -1.0, 1.0, ALU.mult, ALU.add))
    drain("vector")
    run("vector", lambda e: e.tensor_scalar_max(lamcat[0:1, 0:NH],
                                                lamcat[0:1, 0:NH], TINY))
    drain("vector")
    v = inc("vector", cV, lambda e: e.tensor_scalar_min(lamcat[0:1, 0:NH],
                                                        lamcat[0:1, 0:NH],
                                                        LAM_HI))
    wait("scalar", cV, v)
    s = inc("scalar", cS,
            lambda e: e.activation(lamcat[0:1, NH:2 * NH], lamcat[0:1, 0:NH],
                                   AF.Ln))
    wait("tensor", cS, s)
    wait("tensor", cV, v_ones)
    i0 = pA.acquire("tensor")
    t = inc("tensor", cT,
            lambda e, i0=i0: e.matmul(P[i0][:, 0:2 * NH + 1], ones_row[0:1, :],
                                      lamcat[0:1, :], start=True, stop=True))
    wait("vector", cT, t)
    run("vector", lambda e, i0=i0: e.tensor_copy(lam_bc[:], P[i0][:, 0:NH]))
    run("vector", lambda e, i0=i0: e.tensor_copy(llam_bc[:], P[i0][:, NH:2 * NH]))
    v = inc("vector", cV,
            lambda e, i0=i0: e.tensor_copy(osbc[:], P[i0][:, 2 * NH:2 * NH + 1]))
    pA.release(i0, cV, v)
    drain("vector")
    inc("vector", cV,
        lambda e: e.tensor_tensor(dsos[:], dsc[:],
                                  osbc[:, 0:1].broadcast_to((128, NK)),
                                  ALU.mult))
    drain("vector")

    xdt_done_v = None
    if c.PREV:
        NPJ = c.NPJ
        XPW = min(D, 1024, TH)
        nxs = D // XPW
        dwt_all = sc8[:, 0:NPJ * NH].rearrange("p (j h) -> p j h", h=NH)
        dwtb_all = rows[:, 0:(NPJ * NH + 1) // 2].bitcast(BF16)
        jrow = rows[0:1, 2 * TH:2 * TH + NPJ]
        v = inc("vector", cV,
                lambda e: e.tensor_scalar(jrow, iota_row[0:1, 0:NPJ], -128.0,
                                          float(c.PREV - 1), ALU.mult, ALU.add))
        wait("tensor", cV, v)
        i0 = pA.acquire("tensor")
        t = inc("tensor", cT,
                lambda e, i0=i0: e.matmul(P[i0][:, 0:NPJ], ones_row[0:1, :],
                                          jrow, start=True, stop=True))
        wait("vector", cT, t)
        run("vector", lambda e, i0=i0: e.tensor_copy(ecl[:], P[i0][:, 0:NPJ]))
        drain("vector")
        run("vector",
            lambda e: e.tensor_tensor(ecl[:], ecl[:],
                                      iota_col[:, 0:1].broadcast_to((128, NPJ)),
                                      ALU.subtract))
        drain("vector")
        v = inc("vector", cV,
                lambda e: e.tensor_tensor(
                    dwt_all,
                    ecl[:].unsqueeze(2).broadcast_to((128, NPJ, NH)),
                    llam_bc[:].unsqueeze(1).broadcast_to((128, NPJ, NH)),
                    ALU.mult))
        pA.release(i0, cV, v)
        wait("scalar", cV, v)
        s2 = inc("scalar", cS,
                 lambda e: e.activation(sc8[:, 0:NPJ * NH], sc8[:, 0:NPJ * NH],
                                        AF.Exp))
        wait("vector", cS, s2)
        vdw = inc("vector", cV,
                  lambda e: e.tensor_copy(rows[:, 0:(NPJ * NH + 1) // 2]
                                          .bitcast(BF16), sc8[:, 0:NPJ * NH]))
        drain("vector")

        CW = min(512, XPW)
        npc = XPW // CW
        xps_slots = [sc8[:, 0:XPW], sc8[:, TH:TH + XPW],
                     y2a[:, 0:XPW], ret[:, 0:XPW]]
        sP = [Sem(f"sP{i}") for i in range(4)]
        xpf_pool = Pool(4)
        xpb_pool = Pool(min(4, (2 * TH) // CW))
        i2 = pB.acquire("tensor")
        i3 = pB.acquire("tensor") if D > TH else i2
        cst = {"tl": 0}

        def carry_emit_j(j):
            wait("sync", cV, vdw)
            wait("tensor", cV, vdw)
            for sub in range(nxs):
                si = xpf_pool.acquire("sync")
                inc("sync", sP[si],
                    lambda e, si=si, j=j, sub=sub:
                    e.dma_start(out=xps_slots[si],
                                in_=xprev[j * 128:(j + 1) * 128,
                                          sub * XPW:(sub + 1) * XPW]), 16)
                g = sP[si].n
                for p in range(npc):
                    csem = cV
                    wait("vector", sP[si], g)
                    bi = xpb_pool.acquire("vector")
                    v = inc("vector", cV,
                            lambda e, si=si, bi=bi, p=p:
                            e.tensor_copy(gsl[:, bi * CW:(bi + 1) * CW],
                                          xps_slots[si][:, p * CW:
                                                        (p + 1) * CW]))
                    if p == npc - 1:
                        xpf_pool.release(si, csem, v)
                    col0 = sub * XPW + p * CW
                    pi = i2 if col0 < TH else i3
                    po = col0 % TH
                    wait("tensor", csem, v)
                    t = inc("tensor", cT,
                            lambda e, j=j, bi=bi, pi=pi, po=po:
                            e.matmul(P[pi][0:NH, po:po + CW],
                                     dwtb_all[:, j * NH:(j + 1) * NH],
                                     gsl[:, bi * CW:(bi + 1) * CW],
                                     start=(j == 0), stop=(j == NPJ - 1)))
                    xpb_pool.release(bi, cT, t)
                    cst["tl"] = t

        def carry_finish():
            wait("vector", cT, cst["tl"])
            for piece in range((D + TH - 1) // TH):
                w = min(TH, D - piece * TH)
                pi = i2 if piece == 0 else i3
                v = inc("vector", cV,
                        lambda e, piece=piece, w=w, pi=pi:
                        e.tensor_copy(acc[0:NH, piece * TH: piece * TH + w],
                                      P[pi][0:NH, 0:w]))
            pB.release(i2, cV, v)
            if i3 != i2:
                pB.release(i3, cV, v)
            drain("vector")
            wait("tensor", cV, v)
            for k in range(NK):
                pi = pA.acquire("tensor")
                t = inc("tensor", cT,
                        lambda e, k=k, pi=pi:
                        e.transpose(P[pi][:, 0:NH],
                                    acc[0:NH, k * 128:(k + 1) * 128],
                                    ident[0:NH, 0:NH]))
                wait("vector", cT, t)
                v = inc("vector", cV,
                        lambda e, k=k, pi=pi:
                        e.tensor_copy(xdtb[:, k * NH:(k + 1) * NH],
                                      P[pi][:, 0:NH]))
                pA.release(pi, cV, v)
            return v

    for _half in range(2):
        for m in range(NK):
            slab_seq.append((wv[m], D, "cast"))
        if c.PREV and _half == 0:
            for mj in range(NK):
                slab_seq.append((wv[mj], D, "cast"))
        for m in range(NK):
            slab_seq.append((wq[m], D, "cast"))
        for m in range(NK):
            slab_seq.append((wg[m], D, "cast"))
        for m in range(NK):
            slab_seq.append((wo[m], D, "cast"))
        for qh in range(c.QH):
            for mh in range(c.HQC):
                slab_seq.append((upw[qh * c.HQC + mh], D, "sign"))
            for md in range(NK):
                slab_seq.append((dnw[md, qh], c.HQC * 128, "sign"))

    def emit_xtb(T0, dst, gate_ct=None, eng="scalar"):
        csem = cS if eng == "scalar" else cG
        vals = []
        for m in range(NK):
            xi = xst_pool.acquire("gpsimd")
            g = inc("gpsimd", sX[xi],
                    lambda e, xi=xi, m=m, T0=T0:
                    e.dma_start(out=xst[:, xi * XW: xi * XW + TH],
                                in_=xT[m * 128:(m + 1) * 128, T0:T0 + TH]), 16)
            if m == 0 and gate_ct is not None:
                wait(eng, cT, gate_ct)
            wait(eng, sX[xi], g)
            if eng == "scalar":
                v = inc("scalar", cS,
                        lambda e, xi=xi, m=m, dst=dst:
                        e.copy(dst[:, m * TH:(m + 1) * TH],
                               xst[:, xi * XW: xi * XW + TH]))
            else:
                v = inc("gpsimd", cG,
                        lambda e, xi=xi, m=m, dst=dst:
                        e.tensor_copy(dst[:, m * TH:(m + 1) * TH],
                                      xst[:, xi * XW: xi * XW + TH]))
            xst_pool.release(xi, csem, v)
            vals.append((csem, v))
        return vals

    out_final = 0
    hoisted_xtb = None
    for half in range(2):
        T0 = half * TH
        xb = bxh if half % 2 == 0 else bvg
        vb = bvg if half % 2 == 0 else bxh

        if half == 1 and hoisted_xtb is not None:
            xtb_vals = hoisted_xtb
        else:
            xtb_vals = emit_xtb(T0, xb)
        xtb_kwaits = xtb_vals

        for m in range(NK):
            bi, swaits = slab_take()
            pi = pA.acquire("tensor")
            for sem, val in swaits:
                wait("tensor", sem, val)
            t = mm_group(pi, bi, xb, 0, NK,
                         kwaits=xtb_kwaits if m == 0 else None)
            wsb_pool.release(bi, cT, t)
            if c.PREV and half == 0:
                for j in range(m * c.NPJ // NK, (m + 1) * c.NPJ // NK):
                    carry_emit_j(j)
            wait("vector", cT, t)
            v = inc("vector", cV,
                    lambda e, m=m, pi=pi, vb=vb:
                    e.tensor_copy(vb[:, m * TH:(m + 1) * TH], P[pi][:, 0:TH]))
            pA.release(pi, cV, v)

        if c.PREV and half == 0:
            xdt_done_v = carry_finish()
            wait("tensor", cV, xdt_done_v)
            for mj in range(NK):
                bi, swaits = slab_take()
                pi = pB.acquire("tensor")
                for sem, val in swaits:
                    wait("tensor", sem, val)
                t = 0
                for k in range(NK):
                    final = (k == NK - 1)
                    if final:
                        t = inc("tensor", cT,
                                lambda e, bi=bi, k=k, pi=pi:
                                e.matmul(P[pi][:, 0:NH], slab_block(bi, k),
                                         xdtb[:, k * NH:(k + 1) * NH],
                                         start=(k == 0), stop=True))
                    else:
                        run("tensor",
                            lambda e, bi=bi, k=k, pi=pi:
                            e.matmul(P[pi][:, 0:NH], slab_block(bi, k),
                                     xdtb[:, k * NH:(k + 1) * NH],
                                     start=(k == 0), stop=False))
                wsb_pool.release(bi, cT, t)
                wait("vector", cT, t)
                v = inc("vector", cV,
                        lambda e, mj=mj, pi=pi:
                        e.tensor_copy(stc[:, mj:mj + 1], P[pi][:, mj:mj + 1]))
                pB.release(pi, cV, v)
            drain("vector")

        drain("vector")
        if half == 1:
            wait("vector", sO, out_final)
        stats_defer = None
        for m in range(NK):
            bi, swaits = slab_take()
            pi = pA.acquire("tensor")
            for sem, val in swaits:
                wait("tensor", sem, val)
            t = mm_group(pi, bi, xb, 0, NK)
            wsb_pool.release(bi, cT, t)
            if stats_defer is not None:
                stats_defer()
                stats_defer = None
            run("vector", lambda e, m=m, vb=vb:
                e.tensor_tensor_scan(ret[:],
                                     lam_bc[:, m:m + 1].broadcast_to((128, TH)),
                                     vb[:, m * TH:(m + 1) * TH],
                                     stc[:, m:m + 1], ALU.mult, ALU.add))
            drain("vector")
            run("vector", lambda e, m=m:
                e.tensor_copy(stc[:, m:m + 1], ret[:, TH - 1:TH]))
            wait("vector", cT, t)
            v = inc("vector", cV,
                    lambda e, m=m, pi=pi:
                    e.tensor_tensor(acc[:, m * TH:(m + 1) * TH], P[pi][:, 0:TH],
                                    ret[:], ALU.mult))
            pA.release(pi, cV, v)
            drain("vector")
            stats_defer = (lambda m=m, v=v:
                           stats_contrib(2, m, m == 0, m == NK - 1, v))
        stats_defer()
        wait("vector", cT, cT.n)
        rows_v = ln_stats_rows(2)
        bc_v = None

        pend = []

        def g_apply(m, gi, s_sil):
            wait("vector", cV, bc_v)
            run("vector", lambda e, m=m:
                e.tensor_tensor(ret[:], acc[:, m * TH:(m + 1) * TH], r_bc,
                                ALU.mult))
            drain("vector")
            v = inc("vector", cV,
                    lambda e: e.tensor_tensor(ret[:], ret[:], neg_bc, ALU.add))
            wait("scalar", cV, v)
            sg = inc("scalar", cS,
                     lambda e, m=m:
                     e.activation(y2a[:, 0:TH], ret[:], AF.Identity,
                                  bias=bet[:, m:m + 1], scale=gam[:, m:m + 1]))
            wait("vector", cS, sg)
            wait("vector", cS, s_sil)
            v = inc("vector", cV,
                    lambda e, m=m, gi=gi, vb=vb:
                    e.tensor_tensor(vb[:, m * TH:(m + 1) * TH], y2a[:, 0:TH],
                                    gsl[:, gi * TH:(gi + 1) * TH], ALU.mult))
            gsl_pool.release(gi, cV, v)
            drain("vector")

        for m in range(NK):
            bi, swaits = slab_take()
            pi = pA.acquire("tensor")
            for sem, val in swaits:
                wait("tensor", sem, val)
            t = mm_group(pi, bi, xb, 0, NK)
            wsb_pool.release(bi, cT, t)
            gi = gsl_pool.acquire("scalar")
            wait("scalar", cT, t)
            s_sil = inc("scalar", cS,
                        lambda e, gi=gi, pi=pi:
                        e.activation(gsl[:, gi * TH:(gi + 1) * TH],
                                     P[pi][:, 0:TH], AF.Silu))
            pA.release(pi, cS, s_sil)
            pend.append((m, gi, s_sil))
            if m == min(1, NK - 1):
                bc_v = ln_bcast(rows_v)
            if bc_v is not None:
                while pend:
                    g_apply(*pend.pop(0))

        stats_defer = None
        for m in range(NK):
            bi, swaits = slab_take()
            pi = pA.acquire("tensor")
            for sem, val in swaits:
                wait("tensor", sem, val)
            t = mm_group(pi, bi, vb, 0, NK)
            wsb_pool.release(bi, cT, t)
            if stats_defer is not None:
                stats_defer()
                stats_defer = None
            xi = xst_pool.acquire("gpsimd")
            g = inc("gpsimd", sX[xi],
                    lambda e, xi=xi, m=m, T0=T0:
                    e.dma_start(out=xst[:, xi * XW: xi * XW + TH],
                                in_=xT[m * 128:(m + 1) * 128, T0:T0 + TH]), 16)
            wait("vector", sX[xi], g)
            wait("vector", cT, t)
            v = inc("vector", cV,
                    lambda e, m=m, pi=pi, xi=xi:
                    e.tensor_tensor(acc[:, m * TH:(m + 1) * TH], P[pi][:, 0:TH],
                                    xst[:, xi * XW: xi * XW + TH], ALU.add))
            pA.release(pi, cV, v)
            xst_pool.release(xi, cV, v)
            drain("vector")
            stats_defer = (lambda m=m, v=v:
                           stats_contrib(2, m, m == 0, m == NK - 1, v))
        stats_defer()
        wait("vector", cT, cT.n)
        rows_v = ln_stats_rows(2)
        bc_v = ln_bcast(rows_v)

        wait("vector", cV, bc_v)
        sck = []
        for m in range(NK):
            run("vector", lambda e, m=m:
                e.tensor_tensor(ret[:], acc[:, m * TH:(m + 1) * TH], r_bc,
                                ALU.mult))
            drain("vector")
            v = inc("vector", cV,
                    lambda e, m=m:
                    e.tensor_tensor(acc[:, m * TH:(m + 1) * TH], ret[:],
                                    neg_bc, ALU.add))
            drain("vector")
            wait("scalar", cV, v)
            s = inc("scalar", cS,
                    lambda e, m=m, vb=vb:
                    e.copy(vb[:, m * TH:(m + 1) * TH],
                           acc[:, m * TH:(m + 1) * TH]))
            sck.append(s)
        y1_kwaits = [(cS, sck[k]) for k in range(NK)]

        first_ffn = True
        stats_defer = None
        for qh in range(c.QH):
            for mh in range(c.HQC):
                gmh = qh * c.HQC + mh
                bi, swaits = slab_take()
                pi = pA.acquire("tensor")
                for sem, val in swaits:
                    wait("tensor", sem, val)
                t = mm_group(pi, bi, vb, 0, NK,
                             kwaits=y1_kwaits if first_ffn else None)
                first_ffn = False
                wsb_pool.release(bi, cT, t)
                wait("scalar", cT, t)
                s = inc("scalar", cS,
                        lambda e, mh=mh, pi=pi, gmh=gmh, xb=xb:
                        e.activation(xb[:, mh * TH:(mh + 1) * TH],
                                     P[pi][:, 0:TH], AF.Relu,
                                     scale=usc[:, gmh:gmh + 1]))
                pA.release(pi, cS, s)
                if qh == c.QH - 1 and mh == c.HQC - 1:
                    last_up_t = t
            relu_s = s
            wait("tensor", cS, relu_s)
            for md in range(NK):
                bi, swaits = slab_take()
                pi = pB.acquire("tensor")
                for sem, val in swaits:
                    wait("tensor", sem, val)
                t = mm_group(pi, bi, xb, 0, c.HQC)
                wsb_pool.release(bi, cT, t)
                if stats_defer is not None:
                    stats_defer()
                    stats_defer = None
                wait("vector", cT, t)
                v = inc("vector", cV,
                        lambda e, md=md, pi=pi:
                        e.scalar_tensor_tensor(acc[:, md * TH:(md + 1) * TH],
                                               P[pi][:, 0:TH],
                                               dsos[:, md:md + 1],
                                               acc[:, md * TH:(md + 1) * TH],
                                               ALU.mult, ALU.add))
                pB.release(pi, cV, v)
                if qh == c.QH - 1:
                    drain("vector")
                    stats_defer = (lambda md=md, v=v:
                                   stats_contrib(0, md, md == 0, md == NK - 1,
                                                 v))
        stats_defer()
        if half == 0:
            hoisted_xtb = emit_xtb(TH, vb, gate_ct=last_up_t, eng="gpsimd")
        wait("vector", cT, cT.n)
        rows_v = ln_stats_rows(0)
        pA.evt = [(cV, rows_v), (cV, rows_v)]
        bc_v = ln_bcast(rows_v)

        wait("vector", cV, bc_v)
        for m in range(NK):
            run("vector", lambda e, m=m:
                e.tensor_tensor(ret[:], acc[:, m * TH:(m + 1) * TH], r_bc,
                                ALU.mult))
            drain("vector")
            v = inc("vector", cV,
                    lambda e, m=m:
                    e.tensor_tensor(acc[:, m * TH:(m + 1) * TH], ret[:],
                                    neg_bc, ALU.add))
            drain("vector")
            wait("gpsimd", cV, v)
            out_final = inc("gpsimd", sO,
                            lambda e, m=m, T0=T0:
                            e.dma_start(out=OUT[m * 128:(m + 1) * 128,
                                                T0:T0 + TH],
                                        in_=acc[:, m * TH:(m + 1) * TH]), 16)

    emit("gpsimd", lambda e: e.wait_ge(sO.h, sO.n))

    with nc.Block() as block:
        @block.sync
        def _(e):
            for f in cmds["sync"]:
                f(e)

        @block.gpsimd
        def _(e):
            for f in cmds["gpsimd"]:
                f(e)

        @block.vector
        def _(e):
            for f in cmds["vector"]:
                f(e)

        @block.scalar
        def _(e):
            for f in cmds["scalar"]:
                f(e)

        @block.tensor
        def _(e):
            for f in cmds["tensor"]:
                f(e)

    ctx.close()
    return nc


def _block_attn(w):
    D = w.shape[0]
    nk = D // 128
    wT = np.ascontiguousarray(w.T)
    return np.ascontiguousarray(
        wT.reshape(nk, 128, nk, 128).transpose(2, 1, 0, 3).reshape(nk, 128, D))


def _block_up(w, D, HID):
    nk, nmh = D // 128, HID // 128
    wT = np.ascontiguousarray(w.T)
    return np.ascontiguousarray(
        wT.reshape(nk, 128, nmh, 128).transpose(2, 1, 0, 3).reshape(nmh, 128, D))


def _block_dn(w, D, HID, QH):
    nk, hqc = D // 128, HID // 128 // QH
    wT = np.ascontiguousarray(w.T)
    return np.ascontiguousarray(
        wT.reshape(QH, hqc, 128, nk, 128).transpose(3, 0, 2, 1, 4)
        .reshape(nk, QH, 128, hqc * 128))


def make_in_maps(cfg, x, Wq, Wv, Wo, Wg, ln_gamma, ln_beta, beta,
                 up_w, down_w, up_scale, down_scale, out_scale):
    c = cfg
    f = np.float32
    com = {
        "wq": _block_attn(np.asarray(Wq, f)),
        "wv": _block_attn(np.asarray(Wv, f)),
        "wg": _block_attn(np.asarray(Wg, f)),
        "wo": _block_attn(np.asarray(Wo, f)),
        "upw": _block_up(np.asarray(up_w, f), c.D, c.HID),
        "dnw": _block_dn(np.asarray(down_w, f), c.D, c.HID, c.QH),
        "gam": np.ascontiguousarray(np.asarray(ln_gamma, f).reshape(c.NK, 128).T),
        "bet": np.ascontiguousarray(np.asarray(ln_beta, f).reshape(c.NK, 128).T),
        "usc": np.ascontiguousarray(np.asarray(up_scale, f).reshape(c.NMH, 128).T),
        "dsc": np.ascontiguousarray(np.asarray(down_scale, f).reshape(c.NK, 128).T),
        "osc": np.asarray(out_scale, f).reshape(1, 1),
        "betab": np.asarray(beta, f).reshape(1, c.NH),
        "iota": np.arange(128, dtype=f).reshape(128, 1),
        "iotar": np.arange(128, dtype=f).reshape(1, 128),
        "ident": np.ascontiguousarray(np.eye(128, dtype=f)[:, :c.NH]),
    }
    x = np.asarray(x, f)
    in_maps = []
    for core in range(c.NCORES):
        b, q = core // c.QPB, core % c.QPB
        t0 = q * c.TPC
        d = dict(com)
        d["xT"] = np.ascontiguousarray(x[b, t0:t0 + c.TPC, :].T)
        if c.PREV:
            xp = np.zeros((c.PREV, c.D), f)
            if t0:
                xp[c.PREV - t0:] = x[b, :t0]
            d["xprev"] = xp
        in_maps.append(d)
    return in_maps


def gather_out(cfg, results):
    c = cfg
    y = np.empty((c.NB, c.L, c.D), np.float32)
    for core in range(c.NCORES):
        b, q = core // c.QPB, core % c.QPB
        t0 = q * c.TPC
        y[b, t0:t0 + c.TPC, :] = results[core]["OUT"].T
    return y


def kernel(L0_attn_x, L0_attn_Wq, L0_attn_Wv, L0_attn_Wo, L0_attn_Wg,
           L0_attn_ln_gamma, L0_attn_ln_beta, L0_attn_beta,
           L0_ffn_up_weight, L0_ffn_down_weight, L0_ffn_up_scale,
           L0_ffn_down_scale, L0_ffn_output_scale):
    cfg = FULL
    nc = build_program(cfg)
    in_maps = make_in_maps(cfg, L0_attn_x, L0_attn_Wq, L0_attn_Wv, L0_attn_Wo,
                           L0_attn_Wg, L0_attn_ln_gamma, L0_attn_ln_beta,
                           L0_attn_beta, L0_ffn_up_weight, L0_ffn_down_weight,
                           L0_ffn_up_scale, L0_ffn_down_scale,
                           L0_ffn_output_scale)
    res = run_bass_kernel_spmd(nc, in_maps, core_ids=list(range(8)))
    return gather_out(cfg, list(res.results))
